# version 03 (final)
# baseline (speedup 1.0000x reference)
import numpy as np
import ml_dtypes
from contextlib import ExitStack

import concourse.bass as bass
import concourse.mybir as mybir
from concourse.bass_utils import run_bass_kernel_spmd

F32 = mybir.dt.float32
BF16 = mybir.dt.bfloat16
AF = mybir.ActivationFunctionType
ALU = mybir.AluOpType
AX = mybir.AxisListType

B, S, HID = 2, 4096, 2048
NKH, NVH = 4, 8
DK = 128
QC, KC, VC, ZC = 512, 512, 1024, 1024
NCI = 16
CH = 128
NCH = S // CH
BLK = 512
NBLK = S // BLK
EPS = 1e-6
LNEPS = 1e-5
CLAMP = 60.0


class Buf:
    __slots__ = ("name", "writer", "readers", "writer_eng", "fenced")

    def __init__(self, name):
        self.name = name
        self.writer = None
        self.readers = []
        self.writer_eng = None
        self.fenced = False


class Builder:

    def __init__(self):
        self.ops = []
        self.count = {}
        self.dma_toks = set()
        self.bar_all = {}
        self.bar_dma = {}
        self.dma_high = {}

    def barrier(self):
        self.bar_all = dict(self.count)
        self.bar_dma = dict(self.dma_high)

    def op(self, eng, emit, ins=(), outs=(), dma=False, sem=None, inc=None,
           _isfence=False):
        if not _isfence:
            for b in ins:
                we = b.writer_eng
                if we is not None and we != eng and not b.fenced and (
                        we == "pe" or dma):
                    self.fence(we, b)
        w_all = dict(self.bar_all)
        w_dma = dict(self.bar_dma)

        def merge(tok):
            s, v = tok
            w_all[s] = max(w_all.get(s, 0), v)
            if tok in self.dma_toks:
                w_dma[s] = max(w_dma.get(s, 0), v)

        for b in ins:
            if b.writer is not None:
                merge(b.writer)
        for b in outs:
            if b.writer is not None:
                merge(b.writer)
            for t in b.readers:
                merge(t)
        if sem is None:
            sem = eng
        if inc is None:
            inc = 16 if dma else 1
        self.count[sem] = self.count.get(sem, 0) + inc
        tok = (sem, self.count[sem])
        if dma:
            self.dma_toks.add(tok)
            self.dma_high[sem] = self.count[sem]
        wl = list(w_all.items())
        self.ops.append((eng, emit, wl, inc, sem))
        for b in ins:
            b.readers.append(tok)
        for b in outs:
            b.writer = tok
            b.readers = []
            b.writer_eng = eng
            b.fenced = _isfence
        return tok

    def fence(self, eng, buf):
        return self.op(eng, lambda e: e.drain(), ins=[buf], outs=[buf],
                       _isfence=True)


def build(debug=False):
    nc = bass.Bass("TRN2", target_bir_lowering=False)
    bld = Builder()
    bufs = {}

    def Bf(name):
        if name not in bufs:
            bufs[name] = Buf(name)
        return bufs[name]

    def din(name, shape, dt=F32):
        return nc.dram_tensor(name, shape, dt, kind="ExternalInput").ap()

    x_d = din("x", [S, HID])
    xres_d = din("xres", [S // 4, HID])
    wqkvT_d = din("wqkvT", [128, 16 * (QC + KC + VC)])
    convw_d = din("convw", [128, NCI * 4])
    wzT_d = din("wzT", [128, 16 * ZC])
    waT_d = din("waT", [128, 16 * NVH])
    wbT_d = din("wbT", [128, 16 * NVH])
    alog_d = din("alog", [NVH, 1])
    dtb_d = din("dtb", [NVH, 1])
    normw_d = din("normw", [128, DK])
    woutT_d = din("woutT", [128, 8 * HID])
    wgT_d = din("wgT", [128, 16 * 2048])
    wuT_d = din("wuT", [128, 16 * 2048])
    wdT_d = din("wdT", [128, 16 * HID])
    idb_d = din("idbf", [128, 128], BF16)
    idf_d = din("idf", [128, 128])
    maskN_d = din("maskN", [128, 128])
    maskM_d = din("maskM", [128, 128])
    sel8_d = din("sel8", [NVH, NVH * 128], BF16)
    selssq_d = din("selssq", [128, NKH * NVH], BF16)

    out_d = nc.dram_tensor("out", [S // 4, HID], F32, kind="ExternalOutput").ap()

    xnT_dd = nc.dram_tensor("xnT_dd", [16, 128, S], BF16).ap()
    qT_dd = nc.dram_tensor("qT_dd", [NKH, 128, S], BF16).ap()
    kT_dd = nc.dram_tensor("kT_dd", [NKH, 128, S], BF16).ap()
    vT_dd = nc.dram_tensor("vT_dd", [NVH, 128, S], BF16).ap()
    z_dd = nc.dram_tensor("z_dd", [S, ZC], BF16).ap()
    gT_dd = nc.dram_tensor("gT_dd", [16, 128, S], BF16).ap()
    guard_dd = nc.dram_tensor("guard_dd", [128, 512], F32).ap()
    rs_in_t = nc.dram_tensor("rs_in", [S, HID], F32)
    rs_out_t = nc.dram_tensor("rs_out", [S // 4, HID], F32)
    rs_in = rs_in_t.ap()
    rs_out = rs_out_t.ap()

    with ExitStack() as top:
        ent = top.enter_context

        def dbg(name, ap, ins):
            if debug is not True:
                return
            t = nc.dram_tensor("dbg_" + name, list(ap.shape), ap.dtype,
                               kind="ExternalOutput").ap()
            bld.op("sp", lambda e, t=t, ap=ap: e.dma_start(out=t, in_=ap),
                   ins=[Bf(i) for i in ins], dma=True)

        idb_s = ent(nc.sbuf_tensor("idb_s", [128, 128], BF16))
        idf_s = ent(nc.sbuf_tensor("idf_s", [128, 128], F32))
        maskN_s = ent(nc.sbuf_tensor("maskN_s", [128, 128], F32))
        maskM_s = ent(nc.sbuf_tensor("maskM_s", [128, 128], F32))
        sel8_s = ent(nc.sbuf_tensor("sel8_s", [NVH, NVH * 128], BF16))
        selssq_s = ent(nc.sbuf_tensor("selssq_s", [128, NKH * NVH], BF16))
        ones8_s = ent(nc.sbuf_tensor("ones8_s", [NVH, 128], F32))
        nw_tile = ent(nc.sbuf_tensor("nw_tile", [128, DK], F32))
        alog_s = ent(nc.sbuf_tensor("alog_s", [NVH, 1], F32))
        dtb_s = ent(nc.sbuf_tensor("dtb_s", [NVH, 1], F32))
        lam_s = ent(nc.sbuf_tensor("lam_s", [NVH, 1], F32))

        for nm, dst, src in [
            ("idb", idb_s, idb_d), ("idf", idf_s, idf_d),
            ("maskN", maskN_s, maskN_d), ("maskM", maskM_s, maskM_d),
            ("sel8", sel8_s, sel8_d), ("selssq", selssq_s, selssq_d),
            ("nw_tile", nw_tile, normw_d),
            ("alog", alog_s, alog_d), ("dtb", dtb_s, dtb_d),
        ]:
            bld.op("sp", lambda e, d=dst, s=src: e.dma_start(out=d[:, :], in_=s),
                   outs=[Bf("c_" + nm)], dma=True)
        bld.op("dve", lambda e: e.memset(ones8_s[:, :], 1.0), outs=[Bf("ones8")])
        bld.op("act", lambda e: e.activation(lam_s[:, :], alog_s[:, :], AF.Exp),
               ins=[Bf("c_alog")], outs=[Bf("lam")])
        bld.op("dve", lambda e: e.tensor_scalar_mul(lam_s[:, :], lam_s[:, :], -1.0),
               ins=[Bf("lam")], outs=[Bf("lam")])

        dbg("nw_tile", nw_tile[:, :], ["c_nw_tile"])

        with ExitStack() as ph:
            e1 = ph.enter_context
            wqkvT_s = e1(nc.sbuf_tensor("wqkvT_s", [128, 16 * (QC + KC + VC)], BF16))
            convw_s = e1(nc.sbuf_tensor("convw_s", [128, NCI * 4], F32))
            x_st = [e1(nc.sbuf_tensor(f"x_st{i}", [128, HID], F32)) for i in (0, 1)]
            sqs_st = e1(nc.sbuf_tensor("sqs_st", [128, HID], F32))
            mu_st = e1(nc.sbuf_tensor("mu_st", [128, 4], F32))
            xn_b = e1(nc.sbuf_tensor("xn_b", [128, HID], BF16))
            xnT_s = e1(nc.sbuf_tensor("xnT_s", [128, 16 * BLK], BF16))
            conv_rot = [e1(nc.sbuf_tensor(f"conv{i}", [128, BLK + 3], F32)) for i in (0, 1)]
            cacc_d = [e1(nc.sbuf_tensor(f"cacc_d{i}", [128, BLK], F32)) for i in (0, 1)]
            cacc_p = [e1(nc.sbuf_tensor(f"cacc_p{i}", [128, BLK], F32)) for i in (0, 1)]
            stash = e1(nc.sbuf_tensor("stash", [128, NCI * 3], F32))
            qkv_st = e1(nc.sbuf_tensor("qkv_st", [128, NCI * BLK], BF16))
            p_qkv = [e1(nc.psum_tensor(f"p_qkv{i}", [128, BLK], F32)) for i in range(4)]
            p_tr = e1(nc.psum_tensor("p_tr", [128, 1024], BF16))

            bld.op("pool", lambda e: e.dma_start(out=wqkvT_s[:, :], in_=wqkvT_d),
                outs=[Bf("wqkvT")], dma=True)
            bld.op("sp", lambda e: e.dma_start(out=convw_s[:, :], in_=convw_d),
                outs=[Bf("convw")], dma=True)
            bld.op("dve", lambda e: e.memset(stash[:, :], 0.0), outs=[Bf("stash")])

            for blk in range(NBLK):
                xs = x_st[blk % 2]
                xsb = Bf(f"x_st{blk % 2}")
                cv = conv_rot[blk % 2]
                cvb = Bf(f"conv{blk % 2}")
                xnTb = Bf("xnT_s")
                for tt in range(4):
                    t0 = blk * BLK + tt * 128
                    bld.op("sp", lambda e, t0=t0, xs=xs: e.dma_start(
                        out=xs[:, :], in_=x_d[t0:t0 + 128, :]), outs=[xsb], dma=True)
                    bld.op("dve", lambda e, xs=xs: e.reduce_sum(
                        mu_st[:, 0:1], xs[:, :], AX.X), ins=[xsb], outs=[Bf("mu")])
                    bld.op("dve", lambda e: e.tensor_scalar_mul(
                        mu_st[:, 0:1], mu_st[:, 0:1], 1.0 / HID),
                        ins=[Bf("mu")], outs=[Bf("mu")])
                    bld.op("dve", lambda e, xs=xs: e.tensor_scalar_sub(
                        xs[:, :], xs[:, :], mu_st[:, 0:1]),
                        ins=[xsb, Bf("mu")], outs=[xsb])
                    bld.op("act", lambda e, xs=xs: e.activation(
                        sqs_st[:, :], xs[:, :], AF.Square),
                        ins=[xsb], outs=[Bf("sqs")])
                    bld.op("dve", lambda e: e.reduce_sum(
                        mu_st[:, 1:2], sqs_st[:, :], AX.X),
                        ins=[Bf("sqs")], outs=[Bf("var")])
                    bld.op("dve", lambda e: e.tensor_scalar(
                        mu_st[:, 2:3], mu_st[:, 1:2], 1.0 / HID, LNEPS,
                        ALU.mult, ALU.add), ins=[Bf("var")], outs=[Bf("var")])
                    bld.op("act", lambda e: e.activation(
                        mu_st[:, 2:3], mu_st[:, 2:3], AF.Ln),
                        ins=[Bf("var")], outs=[Bf("var")])
                    bld.op("act", lambda e: e.activation(
                        mu_st[:, 3:4], mu_st[:, 2:3], AF.Exp, scale=-0.5),
                        ins=[Bf("var")], outs=[Bf("rstd")])
                    bld.op("dve", lambda e, xs=xs: e.tensor_scalar_mul(
                        xn_b[:, :], xs[:, :], mu_st[:, 3:4]),
                        ins=[xsb, Bf("rstd")], outs=[Bf("xn_b")])
                    for half in range(2):
                        for fi in range(8):
                            f = half * 8 + fi
                            bld.op("pe", lambda e, f=f, fi=fi: e.transpose(
                                p_tr[:, fi * 128:(fi + 1) * 128],
                                xn_b[:, f * 128:(f + 1) * 128], idb_s[:, :]),
                                ins=[Bf("xn_b"), Bf("c_idb")], outs=[Bf("p_tr")])
                        bld.fence("pe", Bf("p_tr"))
                        bld.op("act", lambda e, half=half, tt=tt: e.copy(
                            xnT_s.ap().rearrange("p (f t) -> p f t", t=BLK)
                            [:, half * 8:half * 8 + 8, tt * 128:(tt + 1) * 128],
                            p_tr.ap().rearrange("p (f t) -> p f t", t=128)),
                            ins=[Bf("p_tr")], outs=[xnTb])
                bld.op("sp", lambda e, blk=blk: e.dma_start(
                    out=xnT_dd[:, :, blk * BLK:(blk + 1) * BLK].rearrange("f p t -> p f t"),
                    in_=xnT_s.ap().rearrange("p (f t) -> p f t", t=BLK)),
                    ins=[xnTb], outs=[Bf(f"xnTdd{blk}")], dma=True)

                for ci in range(NCI):
                    pq = p_qkv[ci % 4]
                    pqb = Bf(f"p_qkv{ci % 4}")
                    for f in range(16):
                        bld.op("pe", lambda e, ci=ci, f=f, pq=pq: e.matmul(
                            pq[:, :],
                            wqkvT_s[:, (f * NCI + ci) * 128:(f * NCI + ci) * 128 + 128],
                            xnT_s[:, f * BLK:(f + 1) * BLK],
                            start=(f == 0), stop=(f == 15)),
                            ins=[Bf("wqkvT"), xnTb], outs=[pqb])
                    bld.op("dve", lambda e, cv=cv, ci=ci: e.tensor_copy(
                        cv[:, 0:3], stash[:, ci * 3:ci * 3 + 3]),
                        ins=[Bf("stash")], outs=[cvb])
                    bld.op("dve", lambda e, cv=cv, pq=pq: e.tensor_copy(
                        cv[:, 3:3 + BLK], pq[:, :]), ins=[pqb], outs=[cvb])
                    bld.op("dve", lambda e, cv=cv, ci=ci: e.tensor_copy(
                        stash[:, ci * 3:ci * 3 + 3], cv[:, BLK:BLK + 3]),
                        ins=[cvb], outs=[Bf("stash")])
                    eng = "dve"
                    cacc = cacc_d if ci % 2 == 0 else cacc_p
                    cname = "cacc_d" if ci % 2 == 0 else "cacc_p"
                    bld.op(eng, lambda e, cv=cv, ci=ci, cacc=cacc: e.tensor_scalar_mul(
                        cacc[0][:, :], cv[:, 0:BLK], convw_s[:, ci * 4:ci * 4 + 1]),
                        ins=[cvb, Bf("convw")], outs=[Bf(cname + "0")])
                    for k in (1, 2, 3):
                        bld.op(eng, lambda e, cv=cv, ci=ci, k=k, cacc=cacc:
                               e.scalar_tensor_tensor(
                                   cacc[k % 2][:, :], cv[:, k:k + BLK],
                                   convw_s[:, ci * 4 + k:ci * 4 + k + 1],
                                   cacc[(k + 1) % 2][:, :], ALU.mult, ALU.add),
                               ins=[cvb, Bf("convw"), Bf(cname + str((k + 1) % 2))],
                               outs=[Bf(cname + str(k % 2))])
                    bld.op("act", lambda e, ci=ci, cacc=cacc: e.activation(
                        qkv_st[:, ci * BLK:(ci + 1) * BLK], cacc[1][:, :], AF.Silu),
                        ins=[Bf(cname + "1")], outs=[Bf("qkv_st")])
                for dst, c0, cn in ((qT_dd, 0, 4), (kT_dd, 4, 4), (vT_dd, 8, 8)):
                    bld.op("sp", lambda e, dst=dst, c0=c0, cn=cn, blk=blk: e.dma_start(
                        out=dst[:, :, blk * BLK:(blk + 1) * BLK].rearrange("c p t -> p c t"),
                        in_=qkv_st.ap().rearrange("p (c t) -> p c t", t=BLK)
                        [:, c0:c0 + cn, :]),
                        ins=[Bf("qkv_st")], outs=[Bf(f"qkvdd{blk}_{c0}")], dma=True)
                if debug and blk == 0:
                    dbg("mixed_b0", qkv_st[:, :], ["qkv_st"])
                    dbg("xnT_b0", xnT_s[:, :], ["xnT_s"])

        bld.barrier()

        hm_scope = ExitStack()
        bcum1 = hm_scope.enter_context(nc.sbuf_tensor("bcum1", [NVH, S], F32))
        beta_hm = hm_scope.enter_context(nc.sbuf_tensor("beta_hm", [NVH, S], F32))
        lnb_hm = hm_scope.enter_context(nc.sbuf_tensor("lnb_hm", [NVH, S], F32))
        ab_scope = ExitStack()
        a_hm = ab_scope.enter_context(nc.sbuf_tensor("a_hm", [NVH, S], F32))
        b_hm = ab_scope.enter_context(nc.sbuf_tensor("b_hm", [NVH, S], F32))

        with ExitStack() as ph:
            e1 = ph.enter_context
            wzT_s = e1(nc.sbuf_tensor("wzT_s", [128, 16 * ZC], BF16))
            waT_s = e1(nc.sbuf_tensor("waT_s", [128, 16 * NVH], BF16))
            wbT_s = e1(nc.sbuf_tensor("wbT_s", [128, 16 * NVH], BF16))
            xnT2 = [e1(nc.sbuf_tensor(f"xnT2_{i}", [128, 16 * BLK], BF16)) for i in (0, 1)]
            z_st = e1(nc.sbuf_tensor("z_st", [128, 4 * ZC], BF16))
            p_z = [e1(nc.psum_tensor(f"p_z{i}", [128, 512], F32)) for i in range(4)]
            p_ab = e1(nc.psum_tensor("p_ab", [NVH, 2 * BLK], F32))

            for nm, dst, src in [("wzT", wzT_s, wzT_d), ("waT", waT_s, waT_d),
                                 ("wbT", wbT_s, wbT_d)]:
                bld.op("pool", lambda e, dst=dst, src=src: e.dma_start(
                    out=dst[:, :], in_=src), outs=[Bf(nm)], dma=True)

            for blk in range(NBLK):
                xt = xnT2[blk % 2]
                xtb = Bf(f"xnT2_{blk % 2}")
                bld.op("sp", lambda e, blk=blk, xt=xt: e.dma_start(
                    out=xt.ap().rearrange("p (f t) -> p f t", t=BLK),
                    in_=xnT_dd[:, :, blk * BLK:(blk + 1) * BLK].rearrange(
                        "f p t -> p f t")),
                    ins=[Bf(f"xnTdd{blk}")], outs=[xtb], dma=True)
                for tt in range(4):
                    plo = p_z[(2 * tt) % 4]
                    phi = p_z[(2 * tt + 1) % 4]
                    plob = Bf(f"p_z{(2 * tt) % 4}")
                    phib = Bf(f"p_z{(2 * tt + 1) % 4}")
                    for f in range(16):
                        bld.op("pe", lambda e, f=f, tt=tt, plo=plo, xt=xt: e.matmul(
                            plo[:, :],
                            xt[:, f * BLK + tt * 128:f * BLK + (tt + 1) * 128],
                            wzT_s[:, f * ZC:f * ZC + 512],
                            start=(f == 0), stop=(f == 15)),
                            ins=[Bf("wzT"), xtb], outs=[plob])
                    for f in range(16):
                        bld.op("pe", lambda e, f=f, tt=tt, phi=phi, xt=xt: e.matmul(
                            phi[:, :],
                            xt[:, f * BLK + tt * 128:f * BLK + (tt + 1) * 128],
                            wzT_s[:, f * ZC + 512:(f + 1) * ZC],
                            start=(f == 0), stop=(f == 15)),
                            ins=[Bf("wzT"), xtb], outs=[phib])
                    bld.op("dve", lambda e, tt=tt, plo=plo: e.tensor_copy(
                        z_st[:, tt * ZC:tt * ZC + 512], plo[:, :]),
                        ins=[plob], outs=[Bf("z_st")])
                    bld.op("dve", lambda e, tt=tt, phi=phi: e.tensor_copy(
                        z_st[:, tt * ZC + 512:(tt + 1) * ZC], phi[:, :]),
                        ins=[phib], outs=[Bf("z_st")])
                bld.op("sp", lambda e, blk=blk: e.dma_start(
                    out=z_dd[blk * BLK:(blk + 1) * BLK, :].rearrange(
                        "(tt p) c -> p tt c", p=128),
                    in_=z_st.ap().rearrange("p (tt c) -> p tt c", c=ZC)),
                    ins=[Bf("z_st")], outs=[Bf(f"zdd{blk}")], dma=True)
                for f in range(16):
                    bld.op("pe", lambda e, f=f, xt=xt: e.matmul(
                        p_ab[:, 0:BLK], waT_s[:, f * NVH:(f + 1) * NVH],
                        xt[:, f * BLK:(f + 1) * BLK],
                        start=(f == 0), stop=(f == 15)),
                        ins=[Bf("waT"), xtb], outs=[Bf("p_a")])
                for f in range(16):
                    bld.op("pe", lambda e, f=f, xt=xt: e.matmul(
                        p_ab[:, BLK:2 * BLK], wbT_s[:, f * NVH:(f + 1) * NVH],
                        xt[:, f * BLK:(f + 1) * BLK],
                        start=(f == 0), stop=(f == 15)),
                        ins=[Bf("wbT"), xtb], outs=[Bf("p_b")])
                bld.op("dve", lambda e, blk=blk: e.tensor_copy(
                    a_hm[:, blk * BLK:(blk + 1) * BLK], p_ab[:, 0:BLK]),
                    ins=[Bf("p_a")], outs=[Bf("a_hm")])
                bld.op("dve", lambda e, blk=blk: e.tensor_copy(
                    b_hm[:, blk * BLK:(blk + 1) * BLK], p_ab[:, BLK:2 * BLK]),
                    ins=[Bf("p_b")], outs=[Bf("b_hm")])

        with ExitStack() as ph:
            e1 = ph.enter_context
            bcum0 = e1(nc.sbuf_tensor("bcum0", [NVH, S], F32))
            hmt = e1(nc.sbuf_tensor("hmt", [NVH, S], F32))
            bcum = [bcum0, bcum1]

            def softplus_hm(dst, dstb_name, pre_emit, pre_ins):
                bld.op("dve", pre_emit, ins=pre_ins, outs=[Bf("hmt")])
                bld.op("dve", lambda e: e.tensor_scalar_max(dst[:, :], hmt[:, :], 0.0),
                       ins=[Bf("hmt")], outs=[Bf(dstb_name)])
                bld.op("act", lambda e: e.activation(hmt[:, :], hmt[:, :], AF.Abs),
                       ins=[Bf("hmt")], outs=[Bf("hmt")])
                bld.op("act", lambda e: e.activation(hmt[:, :], hmt[:, :], AF.Exp,
                                                     scale=-1.0),
                       ins=[Bf("hmt")], outs=[Bf("hmt")])
                bld.op("act", lambda e: e.activation(hmt[:, :], hmt[:, :], AF.Ln,
                                                     bias=1.0),
                       ins=[Bf("hmt")], outs=[Bf("hmt")])
                bld.op("dve", lambda e: e.tensor_add(dst[:, :], dst[:, :], hmt[:, :]),
                       ins=[Bf("hmt"), Bf(dstb_name)], outs=[Bf(dstb_name)])

            softplus_hm(a_hm, "g_hm",
                        lambda e: e.tensor_scalar_add(hmt[:, :], a_hm[:, :],
                                                      dtb_s[:, 0:1]),
                        [Bf("a_hm"), Bf("c_dtb")])
            bld.op("dve", lambda e: e.tensor_scalar_mul(a_hm[:, :], a_hm[:, :],
                                                        lam_s[:, 0:1]),
                   ins=[Bf("g_hm"), Bf("lam")], outs=[Bf("g_hm")])
            bld.op("act", lambda e: e.activation(beta_hm[:, :], b_hm[:, :], AF.Sigmoid),
                   ins=[Bf("b_hm")], outs=[Bf("beta_hm")])
            softplus_hm(lnb_hm, "lnb",
                        lambda e: e.tensor_scalar_mul(hmt[:, :], b_hm[:, :], -1.0),
                        [Bf("b_hm")])
            bld.op("dve", lambda e: e.tensor_scalar_mul(lnb_hm[:, :], lnb_hm[:, :],
                                                        -1.0),
                   ins=[Bf("lnb")], outs=[Bf("lnb")])
            bld.op("dve", lambda e: e.tensor_copy(bcum[0][:, :], a_hm[:, :]),
                   ins=[Bf("g_hm")], outs=[Bf("bc0")])
            cur = 0
            for j in (1, 2, 4, 8, 16, 32, 64):
                nxt = 1 - cur
                bld.op("dve", lambda e, j=j, cur=cur, nxt=nxt: e.tensor_tensor(
                    bcum[nxt].ap().rearrange("p (c t) -> p c t", t=CH)[:, :, j:CH],
                    bcum[cur].ap().rearrange("p (c t) -> p c t", t=CH)[:, :, j:CH],
                    bcum[cur].ap().rearrange("p (c t) -> p c t", t=CH)[:, :, 0:CH - j],
                    ALU.add), ins=[Bf(f"bc{cur}")], outs=[Bf(f"bc{nxt}")])
                bld.op("dve", lambda e, j=j, cur=cur, nxt=nxt: e.tensor_copy(
                    bcum[nxt].ap().rearrange("p (c t) -> p c t", t=CH)[:, :, 0:j],
                    bcum[cur].ap().rearrange("p (c t) -> p c t", t=CH)[:, :, 0:j]),
                    ins=[Bf(f"bc{cur}")], outs=[Bf(f"bc{nxt}")])
                cur = nxt
            assert cur == 1
            bc = bcum1
            bcb_name = "bc1"
            if debug:
                dbg("g_hm", a_hm[:, :], ["g_hm"])
                dbg("bcum", bc[:, :], [bcb_name])
                dbg("beta_hm", beta_hm[:, :], ["beta_hm"])
                dbg("lnb_hm", lnb_hm[:, :], ["lnb"])
        ab_scope.close()

        bld.barrier()

        with ExitStack() as ph:
            e1 = ph.enter_context
            woutT_s = e1(nc.sbuf_tensor("woutT_s", [128, 8 * HID], BF16))
            q_fm = [e1(nc.sbuf_tensor(f"q_fm{i}", [128, QC], BF16)) for i in (0, 1)]
            k_fm = [e1(nc.sbuf_tensor(f"k_fm{i}", [128, KC], BF16)) for i in (0, 1)]
            v_fm = [e1(nc.sbuf_tensor(f"v_fm{i}", [128, VC], BF16)) for i in (0, 1)]
            z_ch = [e1(nc.sbuf_tensor(f"z_ch{i}", [128, ZC], BF16)) for i in (0, 1)]
            k_tm = e1(nc.sbuf_tensor("k_tm", [128, KC], BF16))
            v_tm = e1(nc.sbuf_tensor("v_tm", [128, VC], BF16))
            cols_s = e1(nc.sbuf_tensor("cols_s", [128, 64], F32))
            sqb = e1(nc.sbuf_tensor("sqb", [128, 1024], BF16))
            scp = e1(nc.sbuf_tensor("scp", [NVH, 10 * CH], F32))
            spl_w = e1(nc.sbuf_tensor("spl_w", [NVH, 3 * CH], BF16))
            spl_m = e1(nc.sbuf_tensor("spl_m", [NVH, 3 * CH], BF16))
            spl_t = e1(nc.sbuf_tensor("spl_t", [NVH, CH], F32))
            kk_m = e1(nc.sbuf_tensor("kk_m", [128, 128], F32))
            kq_m = e1(nc.sbuf_tensor("kq_m", [128, 128], F32))
            dn_s = e1(nc.sbuf_tensor("dn_s", [128, 128], F32))
            n_b = e1(nc.sbuf_tensor("n_b", [128, 128], BF16))
            nt_b = e1(nc.sbuf_tensor("nt_b", [128, 128], BF16))
            mt_b = e1(nc.sbuf_tensor("mt_b", [128, 128], BF16))
            pw_b = [e1(nc.sbuf_tensor(f"pw_b{i}", [128, 128], BF16)) for i in (0, 1)]
            pwt_b = [e1(nc.sbuf_tensor(f"pwt_b{i}", [128, 128], BF16)) for i in (0, 1)]
            y_b = [e1(nc.sbuf_tensor(f"y_b{i}", [128, 128], BF16)) for i in (0, 1)]
            ks_b = e1(nc.sbuf_tensor("ks_b", [128, 128], BF16))
            bv_b = e1(nc.sbuf_tensor("bv_b", [128, 128], BF16))
            kp_b = e1(nc.sbuf_tensor("kp_b", [128, 128], BF16))
            u_s = e1(nc.sbuf_tensor("u_s", [128, 128], F32))
            w_bb = e1(nc.sbuf_tensor("w_bb", [128, 128], BF16))
            wt_b = e1(nc.sbuf_tensor("wt_b", [128, 128], BF16))
            st_b = e1(nc.sbuf_tensor("st_b", [128, 128], BF16))
            vp_b = e1(nc.sbuf_tensor("vp_b", [128, 128], BF16))
            oi_s = e1(nc.sbuf_tensor("oi_s", [128, 128], F32))
            o_s = e1(nc.sbuf_tensor("o_s", [128, NVH * 128], F32))
            osq_s = e1(nc.sbuf_tensor("osq_s", [128, NVH * 128], F32))
            rstd_s = e1(nc.sbuf_tensor("rstd_s", [128, NVH], F32))
            zs_s = e1(nc.sbuf_tensor("zs_s", [128, ZC], F32))
            oh_b = e1(nc.sbuf_tensor("oh_b", [128, NVH * 128], BF16))
            ohT_b = e1(nc.sbuf_tensor("ohT_b", [128, NVH * 128], BF16))
            part_st = e1(nc.sbuf_tensor("part_st", [128, HID], F32))
            st_f = [e1(nc.sbuf_tensor(f"st_f{i}", [128, NVH * 128], F32)) for i in (0, 1)]
            pA = e1(nc.psum_tensor("pA", [128, 512], F32))
            pB = e1(nc.psum_tensor("pB", [128, 512], F32))
            pC = e1(nc.psum_tensor("pC", [128, 512], F32))
            pT = e1(nc.psum_tensor("pT", [128, 1024], BF16))
            pS = e1(nc.psum_tensor("pS", [128, 512], F32))
            pW = e1(nc.psum_tensor("pW", [128, 1024], F32))

            bld.op("pool", lambda e: e.dma_start(out=woutT_s[:, :], in_=woutT_d),
                outs=[Bf("woutT")], dma=True)
            bld.op("dve", lambda e: e.memset(st_f[0][:, :], 0.0), outs=[Bf("st_f0")])

            nwb = nw_tile.ap().unsqueeze(1).broadcast_to([128, NVH, DK])

            for c in range(NCH):
                par = c % 2
                qf, kf, vf, zc = q_fm[par], k_fm[par], v_fm[par], z_ch[par]
                qfb, kfb, vfb, zcb = (Bf(f"q_fm{par}"), Bf(f"k_fm{par}"),
                                      Bf(f"v_fm{par}"), Bf(f"z_ch{par}"))
                t0 = c * CH
                for dst, dstb, src in ((qf, qfb, qT_dd), (kf, kfb, kT_dd),
                                       (vf, vfb, vT_dd)):
                    bld.op("sp", lambda e, dst=dst, src=src, t0=t0: e.dma_start(
                        out=dst.ap().rearrange("p (h t) -> p h t", t=CH),
                        in_=src[:, :, t0:t0 + CH].rearrange("h p t -> p h t")), outs=[dstb], dma=True)
                bld.op("sp", lambda e, zc=zc, t0=t0: e.dma_start(
                    out=zc[:, :], in_=z_dd[t0:t0 + CH, :]),
                    ins=[Bf(f"zdd{c // 4}")], outs=[zcb], dma=True)
                if debug and c == 0:
                    dbg("vfm_c0", vf[:, :], [f"v_fm{par}"])
                    dbg("qfm_c0", qf[:, :], [f"q_fm{par}"])
                    dbg("zch_c0", zc[:, :], [f"z_ch{par}"])

                for i in range(NKH):
                    bld.op("pe", lambda e, i=i, kf=kf: e.transpose(
                        pT[:, i * 128:(i + 1) * 128], kf[:, i * 128:(i + 1) * 128],
                        idb_s[:, :]), ins=[kfb, Bf("c_idb")], outs=[Bf("pT")])
                for i in range(NKH):
                    bld.op("pe", lambda e, i=i, vf=vf: e.transpose(
                        pT[:, (4 + i) * 128:(5 + i) * 128], vf[:, i * 128:(i + 1) * 128],
                        idb_s[:, :]), ins=[vfb], outs=[Bf("pT")])
                bld.fence("pe", Bf("pT"))
                bld.op("dve", lambda e: e.tensor_copy(k_tm[:, :], pT[:, 0:512]),
                       ins=[Bf("pT")], outs=[Bf("k_tm")])
                bld.op("dve", lambda e: e.tensor_copy(v_tm[:, 0:512], pT[:, 512:1024]),
                       ins=[Bf("pT")], outs=[Bf("v_tm")])
                for i in range(NKH, NVH):
                    bld.op("pe", lambda e, i=i, vf=vf: e.transpose(
                        pT[:, (i - 4) * 128:(i - 3) * 128], vf[:, i * 128:(i + 1) * 128],
                        idb_s[:, :]), ins=[vfb], outs=[Bf("pT")])
                bld.fence("pe", Bf("pT"))
                bld.op("dve", lambda e: e.tensor_copy(v_tm[:, 512:1024], pT[:, 0:512]),
                       ins=[Bf("pT")], outs=[Bf("v_tm")])

                C_LNRQ, C_LNRK, C_WBAR, C_UN, C_UM, C_SCU, C_SCM, C_KP, C_EBC = \
                    [slice(i * CH, (i + 1) * CH) for i in range(9)]
                bld.op("act", lambda e, qf=qf: e.activation(
                    sqb[:, 0:512], qf[:, :], AF.Square), ins=[qfb], outs=[Bf("sqb")])
                bld.op("act", lambda e, kf=kf: e.activation(
                    sqb[:, 512:1024], kf[:, :], AF.Square), ins=[kfb], outs=[Bf("sqb")])
                for kh in range(NKH):
                    bld.op("pe", lambda e, kh=kh: e.matmul(
                        pS[0:NVH, 256:384], selssq_s[:, kh * 8:kh * 8 + 8],
                        sqb[:, kh * 128:(kh + 1) * 128],
                        start=(kh == 0), stop=(kh == NKH - 1)),
                        ins=[Bf("sqb"), Bf("c_selssq")], outs=[Bf("pS")])
                for kh in range(NKH):
                    bld.op("pe", lambda e, kh=kh: e.matmul(
                        pS[0:NVH, 384:512], selssq_s[:, kh * 8:kh * 8 + 8],
                        sqb[:, 512 + kh * 128:512 + (kh + 1) * 128],
                        start=(kh == 0), stop=(kh == NKH - 1)),
                        ins=[Bf("sqb"), Bf("c_selssq")], outs=[Bf("pS")])
                bld.op("dve", lambda e: e.tensor_copy(scp[:, C_LNRQ], pS[0:NVH, 256:384]),
                       ins=[Bf("pS")], outs=[Bf("scp")])
                bld.op("dve", lambda e: e.tensor_copy(scp[:, C_LNRK], pS[0:NVH, 384:512]),
                       ins=[Bf("pS")], outs=[Bf("scp")])
                for sl, extra in ((C_LNRQ, -0.5 * float(np.log(DK))), (C_LNRK, 0.0)):
                    bld.op("act", lambda e, sl=sl: e.activation(
                        scp[:, sl], scp[:, sl], AF.Ln), ins=[Bf("scp")], outs=[Bf("scp")])
                    bld.op("act", lambda e, sl=sl: e.activation(
                        scp[:, sl], scp[:, sl], AF.Exp, scale=0.5),
                        ins=[Bf("scp")], outs=[Bf("scp")])
                    bld.op("dve", lambda e, sl=sl: e.tensor_scalar_max(
                        scp[:, sl], scp[:, sl], EPS), ins=[Bf("scp")], outs=[Bf("scp")])
                    bld.op("act", lambda e, sl=sl: e.activation(
                        scp[:, sl], scp[:, sl], AF.Ln), ins=[Bf("scp")], outs=[Bf("scp")])
                    bld.op("dve", lambda e, sl=sl, extra=extra: e.tensor_scalar(
                        scp[:, sl], scp[:, sl], -1.0, extra, ALU.mult, ALU.add),
                        ins=[Bf("scp")], outs=[Bf("scp")])
                bld.op("dve", lambda e, t0=t0: e.tensor_sub(
                    scp[:, C_WBAR], scp[:, C_LNRK], bc[:, t0:t0 + CH]),
                    ins=[Bf("scp"), Bf(bcb_name)], outs=[Bf("scp")])
                bld.op("dve", lambda e, t0=t0: e.tensor_add(
                    scp[:, C_UN], lnb_hm[:, t0:t0 + CH], bc[:, t0:t0 + CH]),
                    ins=[Bf("scp"), Bf("lnb"), Bf(bcb_name)], outs=[Bf("scp")])
                bld.op("dve", lambda e: e.tensor_add(
                    scp[:, C_UN], scp[:, C_UN], scp[:, C_LNRK]),
                    ins=[Bf("scp")], outs=[Bf("scp")])
                bld.op("dve", lambda e, t0=t0: e.tensor_add(
                    scp[:, C_UM], bc[:, t0:t0 + CH], scp[:, C_LNRQ]),
                    ins=[Bf("scp"), Bf(bcb_name)], outs=[Bf("scp")])
                bld.op("act", lambda e: e.activation(scp[:, C_SCU], scp[:, C_UN],
                                                     AF.Exp),
                       ins=[Bf("scp")], outs=[Bf("scp")])
                bld.op("act", lambda e: e.activation(scp[:, C_SCM], scp[:, C_UM],
                                                     AF.Exp),
                       ins=[Bf("scp")], outs=[Bf("scp")])
                bld.op("dve", lambda e, t0=t0: e.tensor_scalar_add(
                    scp[:, C_KP], scp[:, C_WBAR], bc[:, t0 + CH - 1:t0 + CH]),
                    ins=[Bf("scp"), Bf(bcb_name)], outs=[Bf("scp")])
                bld.op("act", lambda e: e.activation(scp[:, C_KP], scp[:, C_KP],
                                                     AF.Exp),
                       ins=[Bf("scp")], outs=[Bf("scp")])
                bld.op("dve", lambda e, t0=t0: e.tensor_scalar_mul(
                    scp[:, C_EBC], ones8_s[:, :], bc[:, t0 + CH - 1:t0 + CH]),
                    ins=[Bf("ones8"), Bf(bcb_name), Bf("scp")], outs=[Bf("scp")])
                bld.op("act", lambda e: e.activation(scp[:, C_EBC], scp[:, C_EBC],
                                                     AF.Exp),
                       ins=[Bf("scp")], outs=[Bf("scp")])
                for nm, sl_, dst, dn in (("w", C_WBAR, spl_w, "spl_w"),
                                         ("m", C_UM, spl_m, "spl_m")):
                    bld.op("dve", lambda e, sl_=sl_, dst=dst: e.tensor_copy(
                        dst[:, 0:CH], scp[:, sl_]),
                        ins=[Bf("scp")], outs=[Bf(dn)])
                    bld.op("dve", lambda e, sl_=sl_, dst=dst: e.tensor_sub(
                        spl_t[:, :], scp[:, sl_], dst[:, 0:CH]),
                        ins=[Bf("scp"), Bf(dn)], outs=[Bf("spl_t")])
                    bld.op("dve", lambda e, dst=dst: e.tensor_copy(
                        dst[:, CH:2 * CH], spl_t[:, :]),
                        ins=[Bf("spl_t")], outs=[Bf(dn)])
                    bld.op("dve", lambda e, dst=dst: e.tensor_sub(
                        spl_t[:, :], spl_t[:, :], dst[:, CH:2 * CH]),
                        ins=[Bf("spl_t"), Bf(dn)], outs=[Bf("spl_t")])
                    bld.op("dve", lambda e, dst=dst: e.tensor_copy(
                        dst[:, 2 * CH:3 * CH], spl_t[:, :]),
                        ins=[Bf("spl_t")], outs=[Bf(dn)])
                    bld.fence("dve", Bf(dn))
                cols_srcs = [
                    (scp[:, C_UN], "scp"), (scp[:, C_WBAR], "scp"),
                    (scp[:, C_SCU], "scp"), (beta_hm[:, t0:t0 + CH], "beta_hm"),
                    (scp[:, C_KP], "scp"), (scp[:, C_SCM], "scp"),
                    (scp[:, C_EBC], "scp")]
                for gi, (sap, sname) in enumerate(cols_srcs):
                    bld.op("pe", lambda e, gi=gi, sap=sap: e.transpose(
                        pS[:, gi * 8:(gi + 1) * 8], sap, idf_s[0:NVH, 0:NVH]),
                        ins=[Bf(sname), Bf("c_idf")], outs=[Bf("pS")])
                bld.fence("pe", Bf("pS"))
                bld.op("dve", lambda e: e.tensor_copy(cols_s[:, 0:56], pS[:, 0:56]),
                       ins=[Bf("pS")], outs=[Bf("cols")])

                def col(g, h):
                    return cols_s[:, g * 8 + h:g * 8 + h + 1]
                G_UN, G_W, G_SCU, G_BETA, G_KP, G_SCM, G_EBC = range(7)

                for h in range(NVH):
                    kh = h // 2
                    if h % 2 == 0:
                        bld.op("pe", lambda e, kh=kh, kf=kf: e.matmul(
                            pA[:, 0:128], kf[:, kh * 128:(kh + 1) * 128],
                            kf[:, kh * 128:(kh + 1) * 128], start=True, stop=True),
                            ins=[kfb], outs=[Bf("pA")])
                        bld.op("dve", lambda e: e.tensor_mul(
                            kk_m[:, :], pA[:, 0:128], maskN_s[:, :]),
                            ins=[Bf("pA"), Bf("c_maskN")], outs=[Bf("kk_m")])
                        bld.op("pe", lambda e, kh=kh, kf=kf, qf=qf: e.matmul(
                            pA[:, 128:256], kf[:, kh * 128:(kh + 1) * 128],
                            qf[:, kh * 128:(kh + 1) * 128], start=True, stop=True),
                            ins=[kfb, qfb], outs=[Bf("pA")])
                        bld.op("dve", lambda e: e.tensor_mul(
                            kq_m[:, :], pA[:, 128:256], maskM_s[:, :]),
                            ins=[Bf("pA"), Bf("c_maskM")], outs=[Bf("kq_m")])
                    for j in range(3):
                        bld.op("pe", lambda e, h=h, j=j: e.matmul(
                            pC[:, 0:128], sel8_s[:, h * 128:(h + 1) * 128],
                            spl_w[:, j * CH:(j + 1) * CH],
                            start=(j == 0), stop=(j == 2)),
                            ins=[Bf("spl_w"), Bf("c_sel8")], outs=[Bf("pC")])
                    bld.op("dve", lambda e, h=h: e.tensor_scalar(
                        dn_s[:, :], pC[:, 0:128], col(G_UN, h), CLAMP, ALU.add, ALU.min),
                        ins=[Bf("pC"), Bf("cols")], outs=[Bf("dn")])
                    bld.op("act", lambda e: e.activation(dn_s[:, :], dn_s[:, :], AF.Exp),
                           ins=[Bf("dn")], outs=[Bf("dn")])
                    bld.op("dve", lambda e: e.tensor_mul(n_b[:, :], dn_s[:, :],
                                                         kk_m[:, :]),
                           ins=[Bf("dn"), Bf("kk_m")], outs=[Bf("n_b")])
                    if debug and c == 0 and h == 0:
                        dbg("kk_c0", kk_m[:, :], ["kk_m"])
                        dbg("n_c0", n_b[:, :], ["n_b"])
                    for j in range(3):
                        bld.op("pe", lambda e, h=h, j=j: e.matmul(
                            pC[:, 128:256], sel8_s[:, h * 128:(h + 1) * 128],
                            spl_m[:, j * CH:(j + 1) * CH],
                            start=(j == 0), stop=(j == 2)),
                            ins=[Bf("spl_m"), Bf("c_sel8")], outs=[Bf("pC")])
                    bld.op("dve", lambda e, h=h: e.tensor_scalar(
                        dn_s[:, :], pC[:, 128:256], col(G_W, h), CLAMP, ALU.add,
                        ALU.min), ins=[Bf("pC"), Bf("cols")], outs=[Bf("dn")])
                    bld.op("act", lambda e: e.activation(dn_s[:, :], dn_s[:, :], AF.Exp),
                           ins=[Bf("dn")], outs=[Bf("dn")])
                    bld.op("dve", lambda e: e.tensor_mul(mt_b[:, :], dn_s[:, :],
                                                         kq_m[:, :]),
                           ins=[Bf("dn"), Bf("kq_m")], outs=[Bf("mt_b")])
                    if debug and c == 0 and h == 0:
                        dbg("kq_c0", kq_m[:, :], ["kq_m"])
                        dbg("mt_c0", mt_b[:, :], ["mt_b"])
                    bld.op("pe", lambda e: e.transpose(pT[:, 0:128], n_b[:, :],
                                                       idb_s[:, :]),
                           ins=[Bf("n_b")], outs=[Bf("pT")])
                    bld.fence("pe", Bf("pT"))
                    bld.op("dve", lambda e: e.tensor_copy(nt_b[:, :], pT[:, 0:128]),
                           ins=[Bf("pT")], outs=[Bf("nt_b")])
                    bld.op("dve", lambda e: e.tensor_sub(y_b[0][:, :], idb_s[:, :],
                                                         nt_b[:, :]),
                           ins=[Bf("nt_b"), Bf("c_idb")], outs=[Bf("y0")])
                    curN, curNt = ("n_b", n_b), ("nt_b", nt_b)
                    ycur = 0
                    for lvl in range(6):
                        pwp, pwpn = pw_b[lvl % 2], f"pw_b{lvl % 2}"
                        pwtp, pwtpn = pwt_b[lvl % 2], f"pwt_b{lvl % 2}"
                        bld.op("pe", lambda e, a=curNt[1], b2=curN[1]: e.matmul(
                            pA[:, 256:384], a[:, :], b2[:, :], start=True, stop=True),
                            ins=[Bf(curNt[0]), Bf(curN[0])], outs=[Bf("pA")])
                        bld.op("dve", lambda e, pwp=pwp: e.tensor_copy(
                            pwp[:, :], pA[:, 256:384]), ins=[Bf("pA")], outs=[Bf(pwpn)])
                        if lvl < 5:
                            bld.op("pe", lambda e, pwp=pwp: e.transpose(
                                pT[:, 128:256], pwp[:, :], idb_s[:, :]),
                                ins=[Bf(pwpn)], outs=[Bf("pT")])
                        bld.op("pe", lambda e, pwp=pwp, ycur=ycur: e.matmul(
                            pA[:, 384:512], pwp[:, :], y_b[ycur][:, :],
                            start=True, stop=True),
                            ins=[Bf(pwpn), Bf(f"y{ycur}")], outs=[Bf("pA")])
                        if lvl < 5:
                            bld.op("dve", lambda e, pwtp=pwtp: e.tensor_copy(
                                pwtp[:, :], pT[:, 128:256]),
                                ins=[Bf("pT"), Bf("pA")], outs=[Bf(pwtpn)])
                        bld.op("dve", lambda e, ycur=ycur: e.tensor_add(
                            y_b[1 - ycur][:, :], pA[:, 384:512], y_b[ycur][:, :]),
                            ins=[Bf("pA"), Bf(f"y{ycur}")], outs=[Bf(f"y{1 - ycur}")])
                        ycur = 1 - ycur
                        curN, curNt = (pwpn, pwp), (pwtpn, pwtp)
                    Y, Yn = y_b[ycur], f"y{ycur}"
                    if debug and c == 0 and h == 0:
                        dbg("yT_c0", Y[:, :], [Yn])
                    bld.op("dve", lambda e, kh=kh, h=h: e.tensor_scalar_mul(
                        ks_b[:, :], k_tm[:, kh * 128:(kh + 1) * 128], col(G_SCU, h)),
                        ins=[Bf("k_tm"), Bf("cols")], outs=[Bf("ks_b")])
                    bld.op("dve", lambda e, h=h: e.tensor_scalar_mul(
                        bv_b[:, :], v_tm[:, h * 128:(h + 1) * 128], col(G_BETA, h)),
                        ins=[Bf("v_tm"), Bf("cols")], outs=[Bf("bv_b")])
                    bld.op("dve", lambda e, kh=kh, h=h: e.tensor_scalar_mul(
                        kp_b[:, :], k_tm[:, kh * 128:(kh + 1) * 128], col(G_KP, h)),
                        ins=[Bf("k_tm"), Bf("cols")], outs=[Bf("kp_b")])
                    if debug and c == 0 and h == 0:
                        dbg("vtm_c0", v_tm[:, :], ["v_tm"])
                        dbg("bv_c0", bv_b[:, :], ["bv_b"])
                        dbg("ks_c0", ks_b[:, :], ["ks_b"])
                    bld.op("pe", lambda e, Y=Y: e.matmul(
                        pB[:, 0:128], Y[:, :], bv_b[:, :], start=True, stop=True),
                        ins=[Bf(Yn), Bf("bv_b")], outs=[Bf("pB")])
                    bld.op("dve", lambda e: e.tensor_copy(u_s[:, :], pB[:, 0:128]),
                           ins=[Bf("pB")], outs=[Bf("u_s")])
                    if debug and c == 0 and h == 0:
                        dbg("u_c0", u_s[:, :], ["u_s"])
                    bld.op("pe", lambda e, Y=Y: e.matmul(
                        pB[:, 128:256], Y[:, :], ks_b[:, :], start=True, stop=True),
                        ins=[Bf(Yn), Bf("ks_b")], outs=[Bf("pB")])
                    bld.op("dve", lambda e: e.tensor_copy(w_bb[:, :], pB[:, 128:256]),
                           ins=[Bf("pB")], outs=[Bf("w_bb")])
                    bld.op("pe", lambda e: e.transpose(pT[:, 0:128], w_bb[:, :],
                                                       idb_s[:, :]),
                           ins=[Bf("w_bb")], outs=[Bf("pT")])
                    bld.fence("pe", Bf("pT"))
                    bld.op("dve", lambda e: e.tensor_copy(wt_b[:, :], pT[:, 0:128]),
                           ins=[Bf("pT")], outs=[Bf("wt_b")])
                    bld.op("dve", lambda e, h=h, par=par: e.tensor_copy(
                        st_b[:, :], st_f[par][:, h * 128:(h + 1) * 128]),
                        ins=[Bf(f"st_f{par}")], outs=[Bf("st_b")])
                    bld.op("pe", lambda e: e.matmul(
                        pB[:, 256:384], wt_b[:, :], st_b[:, :], start=True, stop=True),
                        ins=[Bf("wt_b"), Bf("st_b")], outs=[Bf("pB")])
                    bld.op("dve", lambda e: e.tensor_tensor(
                        vp_b[:, :], u_s[:, :], pB[:, 256:384], ALU.subtract),
                        ins=[Bf("u_s"), Bf("pB")], outs=[Bf("vp_b")])
                    bld.op("pe", lambda e, kh=kh, qf=qf: e.matmul(
                        pB[:, 384:512], qf[:, kh * 128:(kh + 1) * 128], st_b[:, :],
                        start=True, stop=True),
                        ins=[qfb, Bf("st_b")], outs=[Bf("pB")])
                    bld.op("act", lambda e, h=h: e.activation(
                        oi_s[:, :], pB[:, 384:512], AF.Copy, scale=col(G_SCM, h)),
                        ins=[Bf("pB"), Bf("cols")], outs=[Bf("oi_s")])
                    if debug and c == 0 and h == 0:
                        dbg("oi_c0", oi_s[:, :], ["oi_s"])
                    bld.op("pe", lambda e: e.matmul(
                        pC[:, 256:384], mt_b[:, :], vp_b[:, :], start=True, stop=True),
                        ins=[Bf("mt_b"), Bf("vp_b")], outs=[Bf("pC")])
                    bld.op("dve", lambda e, h=h: e.tensor_add(
                        o_s[:, h * 128:(h + 1) * 128], pC[:, 256:384], oi_s[:, :]),
                        ins=[Bf("pC"), Bf("oi_s")], outs=[Bf("o_s")])
                    bld.op("pe", lambda e: e.matmul(
                        pS[:, 128:256], kp_b[:, :], vp_b[:, :], start=True, stop=True),
                        ins=[Bf("kp_b"), Bf("vp_b")], outs=[Bf("pS")])
                    bld.op("dve", lambda e, h=h, par=par: e.scalar_tensor_tensor(
                        st_f[1 - par][:, h * 128:(h + 1) * 128],
                        st_f[par][:, h * 128:(h + 1) * 128], col(G_EBC, h),
                        pS[:, 128:256], ALU.mult, ALU.add),
                        ins=[Bf(f"st_f{par}"), Bf("pS"), Bf("cols")],
                        outs=[Bf(f"st_f{1 - par}")])

                if debug and c == 0:
                    dbg("oraw_c0", o_s[:, :], ["o_s"])
                bld.op("act", lambda e: e.activation(osq_s[:, :], o_s[:, :], AF.Square),
                       ins=[Bf("o_s")], outs=[Bf("osq")])
                bld.op("dve", lambda e: e.tensor_reduce(
                    rstd_s[:, :], osq_s.ap().rearrange("p (h d) -> p h d", d=128),
                    AX.X, ALU.add), ins=[Bf("osq")], outs=[Bf("rstd8")])
                bld.op("dve", lambda e: e.tensor_scalar(
                    rstd_s[:, :], rstd_s[:, :], 1.0 / DK, EPS, ALU.mult, ALU.add),
                    ins=[Bf("rstd8")], outs=[Bf("rstd8")])
                bld.op("act", lambda e: e.activation(rstd_s[:, :], rstd_s[:, :], AF.Ln),
                       ins=[Bf("rstd8")], outs=[Bf("rstd8")])
                bld.op("act", lambda e: e.activation(rstd_s[:, :], rstd_s[:, :],
                                                     AF.Exp, scale=-0.5),
                       ins=[Bf("rstd8")], outs=[Bf("rstd8")])
                for h in range(NVH):
                    bld.op("dve", lambda e, h=h: e.tensor_scalar_mul(
                        o_s[:, h * 128:(h + 1) * 128], o_s[:, h * 128:(h + 1) * 128],
                        rstd_s[:, h:h + 1]),
                        ins=[Bf("o_s"), Bf("rstd8")], outs=[Bf("o_s")])
                bld.op("dve", lambda e: e.tensor_tensor(
                    o_s.ap().rearrange("p (h d) -> p h d", d=DK),
                    o_s.ap().rearrange("p (h d) -> p h d", d=DK), nwb, ALU.mult),
                    ins=[Bf("o_s"), Bf("c_nw_tile")], outs=[Bf("o_s")])
                bld.op("act", lambda e, zc=zc: e.activation(zs_s[:, :], zc[:, :],
                                                            AF.Silu),
                       ins=[zcb], outs=[Bf("zs")])
                bld.op("dve", lambda e: e.tensor_mul(oh_b[:, :], o_s[:, :], zs_s[:, :]),
                       ins=[Bf("o_s"), Bf("zs")], outs=[Bf("oh_b")])
                for i in range(NVH):
                    bld.op("pe", lambda e, i=i: e.transpose(
                        pT[:, i * 128:(i + 1) * 128],
                        oh_b[:, i * 128:(i + 1) * 128], idb_s[:, :]),
                        ins=[Bf("oh_b")], outs=[Bf("pT")])
                bld.fence("pe", Bf("pT"))
                bld.op("dve", lambda e: e.tensor_copy(ohT_b[:, 0:1024], pT[:, 0:1024]),
                       ins=[Bf("pT")], outs=[Bf("ohT")])
                for half in range(2):
                    for f in range(8):
                        for dc in range(2):
                            bld.op("pe", lambda e, half=half, f=f, dc=dc: e.matmul(
                                pW[:, dc * 512:(dc + 1) * 512],
                                ohT_b[:, f * 128:(f + 1) * 128],
                                woutT_s[:, f * HID + half * 1024 + dc * 512:
                                        f * HID + half * 1024 + (dc + 1) * 512],
                                start=(f == 0), stop=(f == 7)),
                                ins=[Bf("ohT"), Bf("woutT")], outs=[Bf("pW")])
                    bld.op("dve", lambda e, half=half: e.tensor_copy(
                        part_st[:, half * 1024:(half + 1) * 1024], pW[:, 0:1024]),
                        ins=[Bf("pW")], outs=[Bf("part_st")])
                bld.op("sp", lambda e, t0=t0: e.dma_start(
                    out=rs_in[t0:t0 + CH, :], in_=part_st[:, :]),
                    ins=[Bf("part_st")], outs=[Bf(f"rsin{c}")], dma=True)
                bld.op("sp", lambda e, par=par: e.dma_start(
                    out=guard_dd[:, 0:32], in_=st_f[1 - par][:, 0:32]),
                    ins=[Bf(f"st_f{1 - par}")], outs=[Bf("guard")], dma=True)
                bld.op("sp", lambda e: e.dma_start(
                    out=guard_dd[:, 32:64], in_=o_s[:, 0:32]),
                    ins=[Bf("o_s")], outs=[Bf("guard")], dma=True)
                bld.op("sp", lambda e: e.dma_start(
                    out=guard_dd[:, 64:96], in_=oh_b[:, 0:64].bitcast(F32)),
                    ins=[Bf("oh_b")], outs=[Bf("guard")], dma=True)
                bld.op("sp", lambda e: e.dma_start(
                    out=guard_dd[:, 96:104], in_=rstd_s[:, :].bitcast(F32)),
                    ins=[Bf("rstd8")], outs=[Bf("guard")], dma=True)
                bld.op("sp", lambda e: e.dma_start(
                    out=guard_dd[:, 104:160], in_=cols_s[:, 0:56]),
                    ins=[Bf("cols")], outs=[Bf("guard")], dma=True)
                if debug and c in (0, 5, 16, 19):
                    dbg(f"rstd_c{c}", rstd_s[:, :], ["rstd8"])
                    dbg(f"o_c{c}", o_s[:, :], ["o_s"])
                    dbg(f"oh_c{c}", oh_b[:, :], ["oh_b"])
                    dbg(f"part_c{c}", part_st[:, :], ["part_st"])
                    dbg(f"st_c{c}", st_f[1 - par][:, :], [f"st_f{1 - par}"])
                if debug and c == 0:
                    dbg("cols_c0", cols_s[:, :], ["cols"])

        hm_scope.close()
        bld.barrier()

        with ExitStack() as ph:
            e1 = ph.enter_context
            wgT_s = e1(nc.sbuf_tensor("wgT_s", [128, 16 * 2048], BF16))
            wuT_s = e1(nc.sbuf_tensor("wuT_s", [128, 16 * 2048], BF16))
            xnT3 = [e1(nc.sbuf_tensor(f"xnT3_{i}", [128, 16 * BLK], BF16)) for i in (0, 1)]
            gtmp = e1(nc.sbuf_tensor("gtmp", [128, BLK], F32))
            gt_b = e1(nc.sbuf_tensor("gt_b", [128, 2 * BLK], BF16))
            p_g = [e1(nc.psum_tensor(f"p_g{i}", [128, BLK], F32)) for i in (0, 1)]
            p_u = [e1(nc.psum_tensor(f"p_u{i}", [128, BLK], F32)) for i in (0, 1)]

            for nm, dst, src in [("wgT", wgT_s, wgT_d), ("wuT", wuT_s, wuT_d)]:
                bld.op("pool", lambda e, dst=dst, src=src: e.dma_start(
                    out=dst[:, :], in_=src), outs=[Bf(nm)], dma=True)

            for blk in range(NBLK):
                xt = xnT3[blk % 2]
                xtb = Bf(f"xnT3_{blk % 2}")
                bld.op("sp", lambda e, blk=blk, xt=xt: e.dma_start(
                    out=xt.ap().rearrange("p (f t) -> p f t", t=BLK),
                    in_=xnT_dd[:, :, blk * BLK:(blk + 1) * BLK].rearrange(
                        "f p t -> p f t")),
                    ins=[Bf(f"xnTdd{blk}")], outs=[xtb], dma=True)
                for hc in range(16):
                    pg, pu = p_g[hc % 2], p_u[hc % 2]
                    pgb, pub = Bf(f"p_g{hc % 2}"), Bf(f"p_u{hc % 2}")
                    for f in range(16):
                        bld.op("pe", lambda e, f=f, hc=hc, pg=pg, xt=xt: e.matmul(
                            pg[:, :],
                            wgT_s[:, f * 2048 + hc * 128:f * 2048 + (hc + 1) * 128],
                            xt[:, f * BLK:(f + 1) * BLK],
                            start=(f == 0), stop=(f == 15)),
                            ins=[Bf("wgT"), xtb], outs=[pgb])
                    for f in range(16):
                        bld.op("pe", lambda e, f=f, hc=hc, pu=pu, xt=xt: e.matmul(
                            pu[:, :],
                            wuT_s[:, f * 2048 + hc * 128:f * 2048 + (hc + 1) * 128],
                            xt[:, f * BLK:(f + 1) * BLK],
                            start=(f == 0), stop=(f == 15)),
                            ins=[Bf("wuT"), xtb], outs=[pub])
                    bld.op("act", lambda e, pg=pg: e.activation(
                        gtmp[:, :], pg[:, :], AF.Silu), ins=[pgb], outs=[Bf("gtmp")])
                    bld.op("dve", lambda e, hc=hc, pu=pu: e.tensor_mul(
                        gt_b[:, (hc % 2) * BLK:(hc % 2 + 1) * BLK], pu[:, :],
                        gtmp[:, :]),
                        ins=[pub, Bf("gtmp")], outs=[Bf(f"gt{hc % 2}")])
                    bld.op("sp", lambda e, hc=hc, blk=blk: e.dma_start(
                        out=gT_dd[hc, :, blk * BLK:(blk + 1) * BLK],
                        in_=gt_b[:, (hc % 2) * BLK:(hc % 2 + 1) * BLK]),
                        ins=[Bf(f"gt{hc % 2}")], outs=[Bf(f"gtdd{blk}_{hc}")], dma=True)

        bld.barrier()

        if debug == "post":
            tmid = nc.dram_tensor("dbg_rs_mid", [3 * CH, HID], F32,
                                  kind="ExternalOutput").ap()
            bld.op("sp", lambda e, tmid=tmid: e.dma_start(
                out=tmid, in_=rs_in[18 * CH:21 * CH, :]),
                ins=[Bf("rsin18"), Bf("rsin19"), Bf("rsin20")], dma=True)
        with ExitStack() as ph:
            e1 = ph.enter_context
            wdT_s = e1(nc.sbuf_tensor("wdT_s", [128, 16 * HID], BF16))
            gt_s = [e1(nc.sbuf_tensor(f"gt_s{i}", [128, 16 * 256], BF16)) for i in (0, 1)]
            prt_s = [e1(nc.sbuf_tensor(f"prt_s{i}", [128, 2 * HID], F32)) for i in (0, 1)]
            stg_s = [e1(nc.sbuf_tensor(f"stg_s{i}", [128, 2 * HID], F32)) for i in (0, 1)]
            p_d = [e1(nc.psum_tensor(f"p_d{i}", [128, 2048], F32)) for i in (0, 1)]

            bld.op("pool", lambda e: e.dma_start(out=wdT_s[:, :], in_=wdT_d),
                outs=[Bf("wdT")], dma=True)

            for tp in range(16):
                par = tp % 2
                gs, ps, sg = gt_s[par], prt_s[par], stg_s[par]
                gsb, psb, sgb = Bf(f"gt_s{par}"), Bf(f"prt_s{par}"), Bf(f"stg_s{par}")
                t0 = tp * 256
                bld.op("sp", lambda e, gs=gs, t0=t0: e.dma_start(
                    out=gs.ap().rearrange("p (h t) -> p h t", t=256),
                    in_=gT_dd[:, :, t0:t0 + 256].rearrange("h p t -> p h t")),
                    ins=[Bf(f"gtdd{tp // 2}_{hh}") for hh in range(16)],
                    outs=[gsb], dma=True)
                bld.op("sp", lambda e, ps=ps, t0=t0: e.dma_start(
                    out=ps.ap().rearrange("p (tt d) -> p tt d", d=HID),
                    in_=rs_in[t0:t0 + 256, :].rearrange("(tt p) d -> p tt d", p=128)),
                    ins=[Bf(f"rsin{t0 // CH}"), Bf(f"rsin{t0 // CH + 1}")],
                    outs=[psb], dma=True)
                for tt in range(2):
                    pd = p_d[tt]
                    pdb = Bf(f"p_d{tt}")
                    for hc in range(16):
                        for dc in range(4):
                            bld.op("pe", lambda e, hc=hc, dc=dc, tt=tt, gs=gs, pd=pd:
                                   e.matmul(
                                       pd[:, dc * 512:(dc + 1) * 512],
                                       gs[:, hc * 256 + tt * 128:
                                          hc * 256 + (tt + 1) * 128],
                                       wdT_s[:, hc * HID + dc * 512:
                                             hc * HID + (dc + 1) * 512],
                                       start=(hc == 0), stop=(hc == 15)),
                                   ins=[gsb, Bf("wdT")], outs=[pdb])
                    bld.op("dve", lambda e, tt=tt, ps=ps, sg=sg, pd=pd: e.tensor_add(
                        sg[:, tt * HID:(tt + 1) * HID], pd[:, :],
                        ps[:, tt * HID:(tt + 1) * HID]),
                        ins=[pdb, psb], outs=[sgb])
                bld.op("dve", lambda e, sg=sg: e.tensor_copy(sg[:, :], sg[:, :]),
                       ins=[sgb], outs=[sgb])
                bld.op("sp", lambda e, sg=sg, t0=t0: e.dma_start(
                    out=rs_in[t0:t0 + 256, :].rearrange("(tt p) d -> p tt d", p=128),
                    in_=sg.ap().rearrange("p (tt d) -> p tt d", d=HID)),
                    ins=[sgb], outs=[Bf(f"rsfin{tp}")], dma=True)

        if debug:
            dbg("rsin19", rs_in[19 * CH:20 * CH, :], ["rsfin9"])
        if debug == "post":
            for nm, t, deps in [
                ("rs_in_f", rs_in, [f"rsfin{tp}" for tp in range(16)]),
                ("xnT_f", xnT_dd, [f"xnTdd{b}" for b in range(NBLK)]),
                ("qT_f", qT_dd, [f"qkvdd{b}_0" for b in range(NBLK)]),
                ("kT_f", kT_dd, [f"qkvdd{b}_4" for b in range(NBLK)]),
                ("vT_f", vT_dd, [f"qkvdd{b}_8" for b in range(NBLK)]),
                ("z_f", z_dd, [f"zdd{b}" for b in range(NBLK)]),
                ("gT_f", gT_dd, [f"gtdd{b}_{h}" for b in range(NBLK) for h in range(16)]),
            ]:
                ap = t if nm == "rs_in_f" or nm == "z_f" else t
                tt2 = nc.dram_tensor("dbg_" + nm, list(ap.shape), ap.dtype,
                                     kind="ExternalOutput").ap()
                bld.op("sp", lambda e, tt2=tt2, ap=ap: e.dma_start(out=tt2, in_=ap),
                       ins=[Bf(d) for d in deps], dma=True)
        bld.op("pool", lambda e: e.collective_compute(
            "ReduceScatter", ALU.add,
            replica_groups=[[0, 1, 2, 3], [4, 5, 6, 7]],
            ins=[rs_in_t.ap().opt()], outs=[rs_out_t.ap().opt()]),
            ins=[Bf(f"rsfin{tp}") for tp in range(16)],
            outs=[Bf("rsout")], sem="cc", inc=1)

        with ExitStack() as ph:
            e1 = ph.enter_context
            fin_s = [e1(nc.sbuf_tensor(f"fin_s{i}", [128, HID], F32)) for i in (0, 1)]
            xr_s = [e1(nc.sbuf_tensor(f"xr_s{i}", [128, HID], F32)) for i in (0, 1)]
            for i in range(8):
                par = i % 2
                fs, xr = fin_s[par], xr_s[par]
                fsb, xrb = Bf(f"fin{par}"), Bf(f"xr{par}")
                bld.op("sp", lambda e, fs=fs, i=i: e.dma_start(
                    out=fs[:, :], in_=rs_out[i * 128:(i + 1) * 128, :]),
                    ins=[Bf("rsout")], outs=[fsb], dma=True)
                bld.op("sp", lambda e, xr=xr, i=i: e.dma_start(
                    out=xr[:, :], in_=xres_d[i * 128:(i + 1) * 128, :]),
                    outs=[xrb], dma=True)
                bld.op("dve", lambda e, fs=fs, xr=xr: e.tensor_add(
                    fs[:, :], fs[:, :], xr[:, :]), ins=[fsb, xrb], outs=[fsb])
                bld.op("dve", lambda e, fs=fs: e.tensor_copy(fs[:, :], fs[:, :]),
                       ins=[fsb], outs=[fsb])
                bld.op("sp", lambda e, fs=fs, i=i: e.dma_start(
                    out=out_d[i * 128:(i + 1) * 128, :], in_=fs[:, :]),
                    ins=[fsb], outs=[Bf(f"outw{i}")], dma=True)

        with ExitStack() as semstk:
            sems = {}
            for s in list(bld.count.keys()):
                sems[s] = semstk.enter_context(nc.semaphore(f"sem_{s}"))
            with nc.Block() as block:
                engmap = {"sp": block.sync, "pe": block.tensor, "dve": block.vector,
                          "act": block.scalar, "pool": block.gpsimd}

                def mk_body(myeng):
                    def body(eng):
                        waited = {}
                        for (oeng, emit, waits, inc, sem) in bld.ops:
                            if oeng != myeng:
                                continue
                            for (ws, wv) in waits:
                                if waited.get(ws, 0) < wv:
                                    eng.wait_ge(sems[ws], wv)
                                    waited[ws] = wv
                            emit(eng).then_inc(sems[sem], inc)
                        if myeng == "sp":
                            eng.wait_ge(sems["sp"], bld.count["sp"])
                    return body

                for name, dec in engmap.items():
                    dec(mk_body(name))

    return nc


_NC_CACHE = {}


def _get_nc(debug=False):
    key = bool(debug)
    if key not in _NC_CACHE:
        _NC_CACHE[key] = build(debug)
    return _NC_CACHE[key]


def make_in_maps(inputs):
    x = np.asarray(inputs["L0_attn_hidden_states"], np.float32)
    Wqkv = np.asarray(inputs["L0_attn_W_qkv"], np.float32)
    Wz = np.asarray(inputs["L0_attn_W_z"], np.float32)
    Wa = np.asarray(inputs["L0_attn_W_a"], np.float32)
    Wb = np.asarray(inputs["L0_attn_W_b"], np.float32)
    convw = np.asarray(inputs["L0_attn_conv_w"], np.float32)
    alog = np.asarray(inputs["L0_attn_A_log"], np.float32)
    dtb = np.asarray(inputs["L0_attn_dt_bias"], np.float32)
    nw = np.asarray(inputs["L0_attn_norm_w"], np.float32)
    Wout = np.asarray(inputs["L0_attn_W_out"], np.float32)
    Wg = np.asarray(inputs["L0_ffn_w_gate"], np.float32)
    Wu = np.asarray(inputs["L0_ffn_w_up"], np.float32)
    Wd = np.asarray(inputs["L0_ffn_w_down"], np.float32)

    bfe = lambda a: np.ascontiguousarray(a).astype(ml_dtypes.bfloat16)
    cc = np.ascontiguousarray

    def pack(a):
        fi = a.shape[0] // 128
        return cc(a.reshape(fi, 128, -1).transpose(1, 0, 2).reshape(128, -1))
    idn = np.eye(128, dtype=np.float32)
    maskN = np.tril(np.ones((128, 128), np.float32), -1)
    maskM = np.triu(np.ones((128, 128), np.float32), 0)
    sel8 = np.zeros((NVH, NVH * 128), np.float32)
    for h in range(NVH):
        sel8[h, h * 128:(h + 1) * 128] = 1.0
    selssq = np.zeros((128, NKH * NVH), np.float32)
    for kh in range(NKH):
        selssq[:, kh * 8 + 2 * kh] = 1.0
        selssq[:, kh * 8 + 2 * kh + 1] = 1.0

    in_maps = []
    for c in range(8):
        s = c % 4
        beta = c // 4
        kh0, vh0 = 4 * s, 8 * s
        qrows = Wqkv[kh0 * 128:kh0 * 128 + QC]
        krows = Wqkv[2048 + kh0 * 128:2048 + kh0 * 128 + KC]
        vrows = Wqkv[4096 + vh0 * 128:4096 + vh0 * 128 + VC]
        wqkv_rows = np.concatenate([qrows, krows, vrows], 0)
        conv_rows = np.concatenate(
            [convw[kh0 * 128:kh0 * 128 + QC],
             convw[2048 + kh0 * 128:2048 + kh0 * 128 + KC],
             convw[4096 + vh0 * 128:4096 + vh0 * 128 + VC]], 0)
        im = {
            "x": cc(x[beta]),
            "xres": cc(x[beta, s * 1024:(s + 1) * 1024]),
            "wqkvT": pack(cc(wqkv_rows.T)),
            "convw": pack(cc(conv_rows)),
            "wzT": pack(cc(Wz[vh0 * 128:vh0 * 128 + ZC].T)),
            "waT": pack(cc(Wa[vh0:vh0 + NVH].T)),
            "wbT": pack(cc(Wb[vh0:vh0 + NVH].T)),
            "alog": cc(alog[vh0:vh0 + NVH].reshape(NVH, 1)),
            "dtb": cc(dtb[vh0:vh0 + NVH].reshape(NVH, 1)),
            "normw": cc(np.broadcast_to(nw.reshape(1, DK), (128, DK))),
            "woutT": pack(cc(Wout[:, vh0 * 128:vh0 * 128 + VC].T)),
            "wgT": pack(cc(Wg[s * 2048:(s + 1) * 2048].T)),
            "wuT": pack(cc(Wu[s * 2048:(s + 1) * 2048].T)),
            "wdT": pack(cc(Wd[:, s * 2048:(s + 1) * 2048].T)),
            "idbf": bfe(idn),
            "idf": idn,
            "maskN": maskN,
            "maskM": maskM,
            "sel8": bfe(sel8),
            "selssq": bfe(selssq),
        }
        in_maps.append(im)
    return in_maps


def kernel(**inputs):
    nc = _get_nc(False)
    in_maps = make_in_maps(inputs)
    res = run_bass_kernel_spmd(nc, in_maps, core_ids=list(range(8)))
    parts = [np.asarray(res.results[c]["out"], np.float32) for c in range(8)]
    return np.concatenate(parts, axis=0).reshape(B, S, HID)
